# version 05 (final)
# baseline (speedup 1.0000x reference)
import numpy as np
from contextlib import ExitStack

import concourse.bass as bass
import concourse.mybir as mybir
from concourse.bass_utils import run_bass_kernel_spmd

AF = mybir.ActivationFunctionType
ALU = mybir.AluOpType
AX = mybir.AxisListType
F32 = mybir.dt.float32
BF16 = mybir.dt.bfloat16

EK_MODE = "normal"
F2_MODE = "normal"
N_CORES = 8
N_TOK = 4096
D = 1024
NH = 16
DH = 64
M = 256
E = 8
H2 = 2048
NCH = 32
NTL = 8
LN_EPS = 1e-5
K_EPS = 1e-4
C_NORM = 2.0 ** -1.5
DMA = 16


class Builder:
    ENGS = ("sync", "tensor", "vector", "scalar", "gpsimd")

    def __init__(self, nc, stack):
        self.nc = nc
        self.prog = {e: [] for e in self.ENGS}
        self.sems = {}
        self.cnt = {}
        self.events = {}
        for name in ("sd", "gd", "od", "st", "sv", "ss", "svv", "sss", "scc"):
            self.sems[name] = stack.enter_context(nc.semaphore(name))
            self.cnt[name] = 0
        self._wait_state = {}
        self.missing = {}

    def op(self, eng, fn, waits=(), inc=None, k=1, mark=None):
        resolved = []
        for wname in waits:
            if wname is None:
                continue
            if wname not in self.events:
                raise AssertionError(f"wait on unknown event {wname}")
            resolved.append(self.events[wname])
        if inc is not None:
            if inc in ("sd", "gd", "od") and self.cnt[inc] > 0:
                resolved.append((inc, self.cnt[inc]))
            self.cnt[inc] += k
            if mark is not None:
                self.events[mark] = (inc, self.cnt[inc])
        sems = self.sems
        wstate = self._wait_state

        def closure(eng_obj, fn=fn, resolved=tuple(resolved), inc=inc, k=k, eng=eng):
            for (s, v) in resolved:
                key = (eng, s)
                if v > wstate.get(key, 0):
                    eng_obj.wait_ge(sems[s], v)
                    wstate[key] = v
            inst = fn(eng_obj)
            if inc is not None:
                inst.then_inc(sems[inc], k)

        self.prog[eng].append(closure)

    def ev(self, name):
        if name in self.events:
            return name
        self.missing.setdefault(name, 0)
        self.missing[name] += 1
        return None

    def emit(self, block):
        self._wait_state.clear()
        for eng in self.ENGS:
            closures = self.prog[eng]
            if not closures:
                continue

            def run(eng_obj, closures=closures):
                for c in closures:
                    c(eng_obj)

            getattr(block, eng)(run)


def _preload_table(eng, funcs):
    from concourse.hw_specs import get_activation_tables
    tabs = get_activation_tables(eng.bass.m.arch)
    for idx, fset in enumerate(tabs.values()):
        if all(f in fset for f in funcs):
            return eng.add_instruction(
                mybir.InstLoadActFuncSet(
                    name=eng.bass.get_next_instruction_name(),
                    act_func_set_id=idx, ins=[], outs=[],
                )
            )
    raise ValueError(f"no table set with {funcs}")


def build_nc(debug=False, stop=None):
    nc = bass.Bass("TRN2", target_bir_lowering=False)

    x_d = nc.dram_tensor("L0_attn_x", [N_TOK, D], F32, kind="ExternalInput")
    wq_d = nc.dram_tensor("L0_attn_wq", [D, D], F32, kind="ExternalInput")
    wk_d = nc.dram_tensor("L0_attn_wk", [D, D], F32, kind="ExternalInput")
    wv_d = nc.dram_tensor("L0_attn_wv", [D, D], F32, kind="ExternalInput")
    wo_d = nc.dram_tensor("L0_attn_wo", [D, D], F32, kind="ExternalInput")
    bo_d = nc.dram_tensor("L0_attn_bo", [D], F32, kind="ExternalInput")
    pj_d = nc.dram_tensor("L0_attn_proj", [M, DH], F32, kind="ExternalInput")
    wsi_d = nc.dram_tensor("L0_ffn_w_shared_in", [D, H2], F32, kind="ExternalInput")
    wso_d = nc.dram_tensor("L0_ffn_w_shared_out", [D, D], F32, kind="ExternalInput")
    w1_d = nc.dram_tensor("L0_ffn_w1_shared", [D, H2], F32, kind="ExternalInput")
    w2_d = nc.dram_tensor("L0_ffn_w2", [E, D, D], F32, kind="ExternalInput")
    wg_d = nc.dram_tensor("L0_ffn_w_gate", [D, E], F32, kind="ExternalInput")
    gb_d = nc.dram_tensor("L0_ffn_gate_bias", [E], F32, kind="ExternalInput")
    y_d = nc.dram_tensor("y", [N_TOK, D], F32, kind="ExternalOutput")

    x1_scr = nc.dram_tensor("x1_scr", [N_TOK, D], F32, kind="Internal")
    ln2_scr = nc.dram_tensor("ln2_scr", [N_TOK, D], BF16, kind="Internal")
    so1_scr = nc.dram_tensor("so1_scr", [N_TOK, D], F32, kind="Internal")
    hid_scr = nc.dram_tensor("hid_scr", [128, 8, N_TOK], BF16, kind="Internal")
    if debug:
        dbg_q = nc.dram_tensor("dbg_q", [128, 8 * N_TOK], BF16, kind="ExternalOutput")
        dbg_C = nc.dram_tensor("dbg_C", [128, NH * 130], F32, kind="ExternalOutput")
        dbg_ctx = nc.dram_tensor("dbg_ctx", [128, NH * 130], BF16, kind="ExternalOutput")
        dbg_eps = nc.dram_tensor("dbg_eps", [128, NH * 65], F32, kind="ExternalOutput")
        dbg_a = nc.dram_tensor("dbg_a", [128, 1], F32, kind="ExternalOutput")
        dbg_vs = nc.dram_tensor("dbg_vs", [65, 16], F32, kind="ExternalOutput")
        dbg_att = nc.dram_tensor("dbg_att", [128, D], BF16, kind="ExternalOutput")
        dbg_attno = nc.dram_tensor("dbg_attno", [128, D], F32, kind="ExternalOutput")
        dbg_nd = nc.dram_tensor("dbg_nd", [128, NH * 65], F32, kind="ExternalOutput")
        dbg_v = nc.dram_tensor("dbg_v", [128, 2 * NH * 65], BF16, kind="ExternalOutput")
        dbg_ek = nc.dram_tensor("dbg_ek", [128, 2 * M], BF16, kind="ExternalOutput")
    fstat_scr = nc.dram_tensor("fstat_scr", [N_TOK, D], F32, kind="Internal")
    cc_in = nc.dram_tensor("cc_in", [1, 1], F32, kind="Internal")
    cc_out = nc.dram_tensor("cc_out", [1, 1], F32, kind="Internal")
    a_scr = nc.dram_tensor("a_scr", [1, 1], F32, kind="Internal")

    stack = ExitStack()
    with stack:
        b = Builder(nc, stack)
        def _finalize():
            total_od = b.cnt["od"]
            if total_od:
                b.prog["sync"].append(
                    lambda sy, n=total_od, sem=b.sems["od"]: sy.wait_ge(sem, n))
            total_gd = b.cnt["gd"]
            b.prog["sync"].append(
                lambda sy, n=total_gd, sem=b.sems["gd"]: sy.wait_ge(sem, n))
            total_sd = b.cnt["sd"]
            b.prog["sync"].append(
                lambda sy, n=total_sd, sem=b.sems["sd"]: sy.wait_ge(sem, n))
            late = {k: v for k, v in b.missing.items() if k in b.events}
            if late:
                raise AssertionError(f"events requested before creation: {late}")
            block = stack.enter_context(nc.Block())
            b.emit(block)
            return nc

        A = stack.enter_context
        sb = nc.sbuf_tensor

        cpool = A(sb("cpool", [128, 1], F32))
        nc.const_aps.register(F32, LN_EPS, cpool.ap()[:, 0:1])
        idn_bf = A(sb("idn_bf", [128, 128], BF16))
        idn_f = A(sb("idn_f", [128, 128], F32))
        e2 = A(sb("e2", [128, 2], BF16))
        ones_row = A(sb("ones_row", [1, 128], BF16))
        ones_mat = A(sb("ones_mat", [128, 128], BF16))
        ones_tok = A(sb("ones_tok", [128, 1], BF16))
        gb_sb = A(sb("gb_sb", [128, E], F32))
        logits_sb = A(sb("logits_sb", [128, NCH * E], F32))
        wdense_sb = A(sb("wdense_sb", [128, NCH * E], F32))
        kmax_col = A(sb("kmax_col", [128, 1], F32))
        kmtmp = A(sb("kmtmp", [128, 16], F32))
        km2 = A(sb("km2", [128, 1], F32))
        gmax = A(sb("gmax", [1, 1], F32))
        a_sc = A(sb("a_sc", [1, 1], F32))
        a_colg = A(sb("a_colg", [128, 1], F32))

        attn_sc = ExitStack()
        As = attn_sc.enter_context
        projcT2 = As(sb("projcT2", [128, M], BF16))
        pj_sb = As(sb("pj_sb", [128, 2, DH], F32))
        C_sb = As(sb("C_sb", [128, NH * 130], F32))
        CTXbf = As(sb("CTXbf", [128, NH * 130], BF16))
        epscs = As(sb("epscs", [128, NH * 65], F32))
        vs_sb = As(sb("vs_sb", [65, 16], F32))
        VS_bf = As(sb("VS_bf", [1, NH * 65], BF16))
        bo_sb = As(sb("bo_sb", [128, D], F32))
        wo_sb = As(sb("wo_sb", [128, 8, D], BF16))
        q_store = As(sb("q_store", [128, 8 * N_TOK], BF16))

        def g_init_a(g):
            g.memset(cpool.ap(), LN_EPS)
            g.memset(idn_bf.ap(), 0.0)
            g.memset(idn_f.ap(), 0.0)
            g.memset(e2.ap(), 0.0)
            g.memset(ones_row.ap(), 1.0)
            g.memset(ones_mat.ap(), 1.0)
            g.memset(ones_tok.ap(), 1.0)
            g.memset(C_sb.ap(), 0.0)
            g.memset(kmax_col.ap(), -1e30)
            return g.dma_start(pj_sb.ap(), pj_d.ap().rearrange("(mc p) d -> p mc d", p=128))

        b.op("gpsimd", g_init_a, inc="gd", k=DMA, mark="gmem")

        def g_init_b(g):
            g.affine_select(out=idn_bf.ap(), in_=idn_bf.ap(), compare_op=ALU.not_equal,
                            fill=1.0, base=0, pattern=[[-1, 128]], channel_multiplier=1)
            g.affine_select(out=idn_f.ap(), in_=idn_f.ap(), compare_op=ALU.not_equal,
                            fill=1.0, base=0, pattern=[[-1, 128]], channel_multiplier=1)
            g.memset(e2.ap()[0:64, 0:1], 0.0625)
            return g.memset(e2.ap()[64:128, 1:2], 0.0625)

        b.op("gpsimd", g_init_b, waits=["gmem"], inc=None, mark=None)
        b.op("gpsimd", lambda g: g.dma_start(
            pj_sb.ap(), pj_d.ap().rearrange("(mc p) d -> p mc d", p=128)),
            inc="gd", k=DMA, mark="ginit")

        qkv_sc = ExitStack()
        wq_sb = qkv_sc.enter_context(sb("wq_sb", [128, 8, D], BF16))
        wk_sb = qkv_sc.enter_context(sb("wk_sb", [128, 8, D], BF16))
        wv_sb = qkv_sc.enter_context(sb("wv_sb", [128, 8, D], BF16))
        for (dst, src, mk) in ((wq_sb, wq_d, "w_wq"), (wk_sb, wk_d, "w_wk"),
                               (wv_sb, wv_d, "w_wv"), (wo_sb, wo_d, "w_wo")):
            b.op("gpsimd", lambda g, d=dst, s=src: g.dma_start(
                d.ap(), s.ap().rearrange("(kc p) n -> p kc n", p=128)),
                inc="gd", k=DMA, mark=mk)

        b.op("sync", lambda sy: sy.dma_start(
            bo_sb.ap(), bo_d.ap().rearrange("(o d) -> o d", o=1).partition_broadcast(128)),
            waits=["ginit"], inc="sd", k=DMA, mark="bo_bc")
        b.op("sync", lambda sy: sy.dma_start(
            gb_sb.ap(), gb_d.ap().rearrange("(o d) -> o d", o=1).partition_broadcast(128)),
            inc="sd", k=DMA, mark="gb_bc")


        ps0 = ExitStack()
        ppj = ps0.enter_context(nc.psum_tensor("ppj", [64, 128], F32))
        for mc in range(2):
            b.op("tensor",
                 lambda t, mc=mc: t.transpose(ppj.ap(), pj_sb.ap()[:, mc, :], idn_f.ap()),
                 waits=["ginit"] + (["pjev_0_1"] if mc == 1 else []),
                 inc="st", mark=f"pjT_{mc}")
            for half in range(2):
                b.op("scalar",
                     lambda s, mc=mc, half=half: s.activation(
                         projcT2.ap()[64 * half:64 * half + 64, mc * 128:(mc + 1) * 128],
                         ppj.ap(), AF.Copy, bias=0.0, scale=C_NORM),
                     waits=[f"pjT_{mc}"], inc="ss", mark=f"pjev_{mc}_{half}")
        ps0.close()

        s1 = ExitStack()
        S1 = s1.enter_context
        xt = [S1(sb(f"xt{i}", [128, 2, D], F32)) for i in range(2)]
        ln0tm = [S1(sb(f"ln0tm{i}", [128, D], BF16)) for i in range(2)]
        ln0fm = S1(sb("ln0fm", [128, 8 * 512], BF16))
        ktile = S1(sb("ktile", [128, 8 * 512], BF16))
        k2t = S1(sb("k2t", [128, 8 * 512], BF16))
        v_sb = [S1(sb(f"v_sb{i}", [128, NH * 65], BF16)) for i in range(2)]
        ek_sb = [S1(sb(f"ek_sb{i}", [128, M], BF16)) for i in range(2)]
        ekf_sb = [S1(sb(f"ekf_sb{i}", [128, M], F32)) for i in range(2)]
        dgn = [S1(sb(f"dgn{i}", [128, 16], F32)) for i in range(2)]
        mv1 = [S1(sb(f"mv1_{i}", [128, 2], F32)) for i in range(2)]
        st12 = [S1(sb(f"st12_{i}", [128, 12], F32)) for i in range(2)]
        rstd0 = [S1(sb(f"rstd0_{i}", [128, 1], F32)) for i in range(2)]
        nmr0 = [S1(sb(f"nmr0_{i}", [128, 1], F32)) for i in range(2)]

        psA = ExitStack()
        pT = psA.enter_context(nc.psum_tensor("pT", [128, 1024], BF16))
        pQK = [psA.enter_context(nc.psum_tensor(f"pQK{i}", [128, 512], F32)) for i in range(2)]
        pV = [psA.enter_context(nc.psum_tensor(f"pV{i}", [128, 512], F32)) for i in range(2)]
        pCX = [psA.enter_context(nc.psum_tensor(f"pCX{i}", [128, 512], F32)) for i in range(2)]
        pVS = psA.enter_context(nc.psum_tensor("pVS", [65, 16], F32))

        for i in range(2):
            b.op("vector", lambda v, i=i: v.memset(v_sb[i].ap(), 1.0),
                 waits=["ginit"], inc="sv", mark=f"vones{i}")

        for t in range(NTL):
            pt = t % 2
            for hh in range(2):
                ht = 2 * t + hh
                b.op("sync", lambda sy, ht=ht: sy.dma_start(
                    xt[ht % 2].ap(),
                    x_d.ap()[256 * ht:256 * (ht + 1), :].rearrange("(g p) d -> p g d", p=128)),
                    waits=[b.ev(f"ln0tm_{2 * (ht - 2) + 1}")],
                    inc="sd", k=DMA, mark=f"x_{ht}")

            for cc4 in range(4):
                c = 4 * t + cc4
                pc = c % 2
                xbuf = xt[(c // 2) % 2]
                xsl = c % 2
                b.op("vector", lambda v, xbuf=xbuf, xsl=xsl, pc=pc: v.bn_stats(
                    st12[pc].ap()[:, 0:6], xbuf.ap()[:, xsl, 0:512]),
                    waits=[f"x_{c // 2}", b.ev(f"aggr_{c - 2}")], inc="svv", mark=f"bns_{c}")
                b.op("vector", lambda v, xbuf=xbuf, xsl=xsl, pc=pc: v.bn_stats(
                    st12[pc].ap()[:, 6:12], xbuf.ap()[:, xsl, 512:1024]),
                    inc="svv", mark=f"bns2_{c}")
                b.op("vector", lambda v, pc=pc: v.bn_aggr(mv1[pc].ap(), st12[pc].ap()),
                     waits=[f"bns2_{c}", b.ev(f"nmr0_{c - 2}")], inc="sv", mark=f"aggr_{c}")
                b.op("scalar", lambda s, pc=pc: s.activation(
                    rstd0[pc].ap(), mv1[pc].ap()[:, 1:2], AF.Ln, bias=LN_EPS, scale=1.0),
                    waits=[f"aggr_{c}", b.ev(f"ln0tm_{c - 2}")], inc="sss", mark=f"lnv_{c}")
                b.op("scalar", lambda s, pc=pc: s.activation(
                    rstd0[pc].ap(), rstd0[pc].ap(), AF.Exp, bias=0.0, scale=-0.5),
                    waits=[f"lnv_{c}"], inc="ss", mark=f"rstd0_{c}")
                b.op("vector", lambda v, pc=pc: v.tensor_scalar(
                    nmr0[pc].ap(), mv1[pc].ap()[:, 0:1], rstd0[pc].ap(), -1.0,
                    ALU.mult, ALU.mult),
                    waits=[f"rstd0_{c}", f"aggr_{c}"], inc="sv", mark=f"nmr0_{c}")
                b.op("scalar", lambda s, xbuf=xbuf, xsl=xsl, pc=pc: s.activation(
                    ln0tm[pc].ap(), xbuf.ap()[:, xsl, :], AF.Identity,
                    bias=nmr0[pc].ap(), scale=rstd0[pc].ap()),
                    waits=[f"nmr0_{c}"], inc="ss", mark=f"ln0tm_{c}")
                for kc in range(8):
                    b.op("tensor", lambda t_, pc=pc, kc=kc: t_.transpose(
                        pT.ap()[:, kc * 128:(kc + 1) * 128],
                        ln0tm[pc].ap()[:, kc * 128:(kc + 1) * 128], idn_bf.ap()),
                        waits=([f"ln0tm_{c}", b.ev(f"ln0fm_{c - 1}"), "pjev_1_1"]
                               if kc == 0 else []),
                        inc="st", mark=(f"T_{c}" if kc == 7 else None))
                b.op("scalar", lambda s, pt=pt, cc4=cc4: s.activation(
                    ln0fm.ap().rearrange("p (kc w) -> p kc w", kc=8)[:, :, cc4 * 128:(cc4 + 1) * 128],
                    pT.ap().rearrange("p (kc w) -> p kc w", kc=8),
                    AF.Copy, bias=0.0, scale=1.0),
                    waits=[f"T_{c}", b.ev(f"qkps_{t - 1}_15"),
                           b.ev(f"vps_{4 * (t - 1) + 3}_1")],
                    inc="ss", mark=f"ln0fm_{c}")

            if stop == f"lnT{t}":
                psA.close()
                s1.close()
                qkv_sc.close()
                attn_sc.close()
                return _finalize()

            for s_idx in range(16):
                is_q = s_idx < 8
                fo = s_idx % 8
                pb = s_idx % 2
                wsb = wq_sb if is_q else wk_sb
                wev = "w_wq" if is_q else "w_wk"
                first_waits = [f"ln0fm_{4 * t + 3}", wev,
                               b.ev(f"qkfree_{t}_{s_idx - 2}")]
                if s_idx < 2:
                    first_waits += [b.ev(f"ek_{4 * t - 1}_15"),
                                    b.ev(f"kmr_{4 * t - 1}_15")]
                for kc in range(8):
                    b.op("tensor", lambda t_, pt=pt, fo=fo, pb=pb, kc=kc, wsb=wsb: t_.matmul(
                        pQK[pb].ap(),
                        wsb.ap()[:, kc, fo * 128:(fo + 1) * 128],
                        ln0fm.ap()[:, kc * 512:(kc + 1) * 512],
                        start=(kc == 0), stop=(kc == 7)),
                        waits=(first_waits if kc == 0 else []),
                        inc="st", mark=(f"qkps_{t}_{s_idx}" if kc == 7 else None))
                if is_q:
                    b.op("scalar", lambda s, t=t, fo=fo, pb=pb: s.activation(
                        q_store.ap()[:, fo * N_TOK + t * 512: fo * N_TOK + (t + 1) * 512],
                        pQK[pb].ap(), AF.Copy, bias=0.0, scale=1.0),
                        waits=[f"qkps_{t}_{s_idx}"], inc="ss", mark=f"qkfree_{t}_{s_idx}")
                else:
                    b.op("scalar", lambda s, pt=pt, fo=fo, pb=pb: s.activation(
                        ktile.ap()[:, fo * 512:(fo + 1) * 512],
                        pQK[pb].ap(), AF.Copy, bias=0.0, scale=1.0),
                        waits=[f"qkps_{t}_{s_idx}"],
                        inc="ss", mark=f"kevac_{t}_{fo}")
                    b.op("scalar", lambda s, pt=pt, fo=fo, pb=pb: s.activation(
                        k2t.ap()[:, fo * 512:(fo + 1) * 512],
                        pQK[pb].ap(), AF.Square, bias=0.0, scale=1.0),
                        waits=[b.ev(f"dgmm_{4 * (t - 1) + 3}")],
                        inc="ss", mark=f"qkfree_{t}_{s_idx}")

            if stop == f"qk{t}":
                psA.close()
                s1.close()
                qkv_sc.close()
                attn_sc.close()
                return _finalize()

            for cc4 in range(4):
                c = 4 * t + cc4
                pc = c % 2
                for half in range(2):
                    for kc in range(8):
                        b.op("tensor", lambda t_, cc4=cc4, half=half, kc=kc: t_.matmul(
                            pV[half].ap(),
                            ln0fm.ap()[:, kc * 512 + cc4 * 128: kc * 512 + (cc4 + 1) * 128],
                            wv_sb.ap()[:, kc, half * 512:(half + 1) * 512],
                            start=(kc == 0), stop=(kc == 7)),
                            waits=([f"ln0fm_{c}", "w_wv",
                                    b.ev(f"vevac_{c - 1}_{half}")] if kc == 0 else []),
                            inc="st", mark=(f"vps_{c}_{half}" if kc == 7 else None))
                    b.op("scalar", lambda s, c=c, half=half: s.activation(
                        v_sb[c % 2].ap().rearrange("p (h w) -> p h w", h=NH)[:, half * 8:(half + 1) * 8, 0:64],
                        pV[half].ap().rearrange("p (h w) -> p h w", h=8),
                        AF.Copy, bias=0.0, scale=1.0),
                        waits=[f"vps_{c}_{half}", f"vones{c % 2}",
                               b.ev(f"ctxmm_{c - 2}_15"), b.ev(f"vsmm_{c - 2}")],
                        inc="ss", mark=f"vevac_{c}_{half}")
                if stop == "v0":
                    continue
                for fc in range(8):
                    b.op("tensor", lambda t_, pt=pt, pc=pc, fc=fc, cc4=cc4: t_.matmul(
                        pQK[pc].ap()[:, 256 + 2 * fc: 258 + 2 * fc],
                        k2t.ap()[:, fc * 512 + cc4 * 128: fc * 512 + (cc4 + 1) * 128],
                        e2.ap(), start=True, stop=True),
                        waits=([f"qkfree_{t}_15", f"kevac_{t}_7",
                                b.ev(f"ek_{c - 1}_{14 + pc}"),
                                b.ev(f"kmr_{c - 1}_{14 + pc}")] if fc == 0 else []),
                        inc="st", mark=(f"dgmm_{c}" if fc == 7 else None))
                b.op("vector", lambda v, pc=pc: v.tensor_scalar_mul(
                    dgn[pc].ap(), pQK[pc].ap()[:, 256:272], -1.0),
                    waits=[f"dgmm_{c}", b.ev(f"ek_{c - 2}_15")],
                    inc="sv", mark=f"negdiag_{c}")
                if stop == "dg0":
                    continue
                for h in range(NH):
                    ph = h % 2
                    hb = 64 * (h % 2)
                    fo = h // 2
                    b.op("tensor", lambda t_, pt=pt, ph=ph, hb=hb, fo=fo, cc4=cc4: t_.matmul(
                        pQK[ph].ap()[:, 0:256],
                        ktile.ap()[hb:hb + 64, fo * 512 + cc4 * 128: fo * 512 + (cc4 + 1) * 128],
                        projcT2.ap()[hb:hb + 64, :], start=True, stop=True),
                        waits=[f"negdiag_{c}",
                               b.ev(f"ek_{c}_{h - 2}"), b.ev(f"kmr_{c}_{h - 2}")],
                        inc="st", mark=f"kd_{c}_{h}")
                    b.op("vector", lambda v, ph=ph, h=h: v.reduce_max(
                        kmtmp.ap()[:, h:h + 1], pQK[ph].ap()[:, 0:256], AX.X),
                        waits=[f"kd_{c}_{h}"] +
                              ([b.ev(f"km2_{c - 1}")] if h == 0 else []),
                        inc="sv", mark=f"kmr_{c}_{h}")
                    if stop == "kdm0":
                        continue
                    if EK_MODE == "normal":
                        ekfn = lambda s, ph=ph, pc=pc, h=h: s.activation(
                            ek_sb[ph].ap(), pQK[ph].ap()[:, 0:256], AF.Exp,
                            bias=dgn[pc].ap()[:, h:h + 1], scale=1.0)
                    elif EK_MODE == "f32":
                        ekfn = lambda s, ph=ph, pc=pc, h=h: s.activation(
                            ekf_sb[ph].ap(), pQK[ph].ap()[:, 0:256], AF.Exp,
                            bias=dgn[pc].ap()[:, h:h + 1], scale=1.0)
                    elif EK_MODE == "nobias":
                        ekfn = lambda s, ph=ph, h=h: s.activation(
                            ek_sb[ph].ap(), pQK[ph].ap()[:, 0:256], AF.Exp,
                            bias=0.0, scale=1.0)
                    else:
                        ekfn = lambda s, ph=ph, pc=pc, h=h: s.activation(
                            ek_sb[ph].ap(), pQK[ph].ap()[:, 0:256], AF.Copy,
                            bias=0.0, scale=1.0)
                    b.op("scalar", ekfn,
                         waits=[f"kd_{c}_{h}", f"kmr_{c}_{h}", f"negdiag_{c}",
                                b.ev(f"ctxmm_{c}_{h - 2}")],
                         inc="ss", mark=f"ek_{c}_{h}")
                    if stop == "kd0":
                        continue
                    for mc in range(2):
                        b.op("tensor", lambda t_, ph=ph, pc=pc, h=h, mc=mc: t_.matmul(
                            pCX[ph].ap()[:, mc * 65:(mc + 1) * 65],
                            ek_sb[ph].ap()[:, mc * 128:(mc + 1) * 128],
                            v_sb[pc].ap()[:, h * 65: h * 65 + 65],
                            start=True, stop=True),
                            waits=([f"ek_{c}_{h}", f"vevac_{c}_{h // 8}",
                                    b.ev(f"ctxadd_{c}_{h - 2}")] if mc == 0 else []),
                            inc="st", mark=(f"ctxmm_{c}_{h}" if mc == 1 else None))
                    b.op("vector", lambda v, ph=ph, h=h: v.tensor_add(
                        C_sb.ap()[:, h * 130: h * 130 + 130],
                        C_sb.ap()[:, h * 130: h * 130 + 130],
                        pCX[ph].ap()[:, 0:130]),
                        waits=[f"ctxmm_{c}_{h}", b.ev(f"ctxadd_{c - 1}_{h}")],
                        inc="sv", mark=f"ctxadd_{c}_{h}")
                    b.op("tensor", lambda t_, pc=pc, h=h, c=c: t_.matmul(
                        pVS.ap()[:, h:h + 1],
                        v_sb[pc].ap()[:, h * 65: h * 65 + 65],
                        ones_tok.ap(),
                        start=(c == 0 and h == 0),
                        stop=(c == NCH - 1 and h == NH - 1)),
                        waits=[f"vevac_{c}_{h // 8}"],
                        inc="st", mark=(f"vsmm_{c}" if h == NH - 1 else None))
                b.op("vector", lambda v: v.reduce_max(km2.ap(), kmtmp.ap(), AX.X),
                     waits=[f"kmr_{c}_{NH - 1}"], inc="svv", mark=f"km2_{c}")
                b.op("vector", lambda v: v.tensor_max(
                    kmax_col.ap(), kmax_col.ap(), km2.ap()),
                    waits=[f"km2_{c}", b.ev(f"kmaxred_{c - 1}")],
                    inc="sv", mark=f"kmaxred_{c}")
            if stop in ("v0", "kd0", "dg0", "kdm0") or stop == f"tile{t}":
                psA.close()
                s1.close()
                qkv_sc.close()
                attn_sc.close()
                return _finalize()

        if stop == "s1":
            psA.close()
            s1.close()
            qkv_sc.close()
            attn_sc.close()
            return _finalize()

        b.op("tensor", lambda t_: t_.transpose(
            pQK[0].ap()[0:1, 0:128], kmax_col.ap(), idn_f.ap()),
            waits=[f"kmaxred_{NCH - 1}", f"ek_{NCH - 1}_15"], inc="st", mark="kmT")
        b.op("vector", lambda v: v.reduce_max(gmax.ap(), pQK[0].ap()[0:1, 0:128], AX.X),
             waits=["kmT"], inc="sv", mark="gm")
        b.op("gpsimd", lambda g: g.dma_start(cc_in.ap(), gmax.ap()),
             waits=["gm"], inc="gd", k=DMA, mark="ccin")
        b.op("gpsimd", lambda g: g.collective_compute(
            "AllReduce", ALU.max, replica_groups=[list(range(N_CORES))],
            ins=[cc_in.ap()], outs=[cc_out.ap()]),
            waits=["ccin"], inc="scc", k=1, mark="cc")
        b.op("gpsimd", lambda g: g.dma_start(gmax.ap(), cc_out.ap()),
             waits=["cc"], inc="gd", k=DMA, mark="ccback")
        b.op("scalar", lambda s: s.activation(
            a_sc.ap(), gmax.ap(), AF.Exp, bias=0.0, scale=-1.0),
            waits=["ccback"], inc="ss", mark="asc")
        b.op("sync", lambda sy: sy.dma_start(a_scr.ap(), a_sc.ap()),
             waits=["asc"], inc="sd", k=DMA, mark="acolw")
        b.op("sync", lambda sy: sy.dma_start(
            a_colg.ap(), a_scr.ap().partition_broadcast(128)),
            waits=["acolw"], inc="sd", k=DMA, mark="acolbc")
        b.op("scalar", lambda s: s.activation(
            vs_sb.ap(), pVS.ap(), AF.Copy, bias=0.0, scale=1.0),
            waits=[f"vsmm_{NCH - 1}"], inc="ss", mark="vsev")
        for h in range(NH):
            b.op("gpsimd", lambda g, h=h: g.dma_start(
                VS_bf.ap()[0:1, h * 65:(h + 1) * 65], vs_sb.ap()[:, h:h + 1]),
                waits=["vsev"], inc="gd", k=DMA, mark=f"vsr_{h}")
        b.op("vector", lambda v: v.tensor_scalar_mul(VS_bf.ap(), VS_bf.ap(), K_EPS),
             waits=[f"vsr_{NH - 1}"], inc="sv", mark="vsbf")
        for h in range(NH):
            ph = h % 2
            b.op("tensor", lambda t_, ph=ph, h=h: t_.matmul(
                pCX[ph].ap()[:, 0:65], ones_row.ap(),
                VS_bf.ap()[0:1, h * 65:(h + 1) * 65], start=True, stop=True),
                waits=["vsbf", f"ctxadd_{NCH - 1}_15",
                       b.ev(f"ctxs_{h - 2}_1")],
                inc="st", mark=f"vsb_{h}")
            for mc in range(2):
                b.op("vector", lambda v, ph=ph, h=h, mc=mc: v.scalar_tensor_tensor(
                    C_sb.ap()[:, h * 130 + mc * 65: h * 130 + (mc + 1) * 65],
                    C_sb.ap()[:, h * 130 + mc * 65: h * 130 + (mc + 1) * 65],
                    a_colg.ap(), pCX[ph].ap()[:, 0:65], ALU.mult, ALU.add),
                    waits=[f"vsb_{h}", "acolbc"], inc="sv", mark=f"ctxs_{h}_{mc}")
        b.op("vector", lambda v: v.tensor_copy(CTXbf.ap(), C_sb.ap()),
             waits=[f"ctxs_{NH - 1}_1"], inc="sv", mark="ctxbf")
        for h in range(NH):
            ph = h % 2
            for mc in range(2):
                b.op("tensor", lambda t_, ph=ph, h=h, mc=mc: t_.matmul(
                    pQK[ph].ap()[:, 0:65], ones_mat.ap(),
                    CTXbf.ap()[:, h * 130 + mc * 65: h * 130 + (mc + 1) * 65],
                    start=(mc == 0), stop=(mc == 1)),
                    waits=(["ctxbf", "gm", b.ev(f"epscs_{h - 2}")] if mc == 0 else []),
                    inc="st", mark=(f"csmm_{h}" if mc == 1 else None))
            b.op("scalar", lambda s, ph=ph, h=h: s.activation(
                epscs.ap()[:, h * 65:(h + 1) * 65], pQK[ph].ap()[:, 0:65],
                AF.Copy, bias=0.0, scale=K_EPS),
                waits=[f"csmm_{h}"], inc="ss", mark=f"epscs_{h}")

        if debug:
            for i in range(2):
                b.op("sync", lambda sy, i=i: sy.dma_start(
                    dbg_v.ap()[:, i * NH * 65:(i + 1) * NH * 65], v_sb[i].ap()),
                    waits=[f"vsmm_{NCH - 1}"], inc="od", k=DMA, mark=f"dbgv{i}")
                b.op("sync", lambda sy, i=i: sy.dma_start(
                    dbg_ek.ap()[:, i * M:(i + 1) * M], ek_sb[i].ap()),
                    waits=[f"ek_{NCH - 1}_15"], inc="od", k=DMA, mark=f"dbgek{i}")
            b.op("sync", lambda sy: sy.dma_start(dbg_q.ap(), q_store.ap()),
                 waits=["qkfree_7_7"], inc="od", k=DMA, mark="dbgq")
            b.op("sync", lambda sy: sy.dma_start(dbg_C.ap(), C_sb.ap()),
                 waits=[f"ctxs_{NH - 1}_1"], inc="od", k=DMA, mark="dbgC")
            b.op("sync", lambda sy: sy.dma_start(dbg_ctx.ap(), CTXbf.ap()),
                 waits=["ctxbf"], inc="od", k=DMA, mark="dbgctx")
            b.op("sync", lambda sy: sy.dma_start(dbg_eps.ap(), epscs.ap()),
                 waits=[f"epscs_{NH - 1}"], inc="od", k=DMA, mark="dbgeps")
            b.op("sync", lambda sy: sy.dma_start(dbg_a.ap(), a_colg.ap()),
                 waits=["acolbc"], inc="od", k=DMA, mark="dbga")
            b.op("sync", lambda sy: sy.dma_start(dbg_vs.ap(), vs_sb.ap()),
                 waits=["vsev"], inc="od", k=DMA, mark="dbgvs")

        if stop == "ctx":
            psA.close()
            s1.close()
            qkv_sc.close()
            attn_sc.close()
            return _finalize()

        s1.close()
        psA.close()
        qkv_sc.close()

        f1wA = ExitStack()
        wsi_sb = f1wA.enter_context(sb("wsi_sb", [128, 8, H2], BF16, side="right"))
        wso_sb = f1wA.enter_context(sb("wso_sb", [128, 8, D], BF16, side="right"))
        dbg_waits = (["dbgv0", "dbgv1", "dbgek0", "dbgek1", "dbgq", "dbgC",
                      "dbgctx", "dbgeps", "dbgvs"] if debug else [])
        for (dst, src, mk) in ((wsi_sb, wsi_d, "w_wsi"), (wso_sb, wso_d, "w_wso")):
            b.op("gpsimd", lambda g, d=dst, s=src: g.dma_start(
                d.ap(), s.ap().rearrange("(kc p) n -> p kc n", p=128)),
                waits=["ctxbf", f"epscs_{NH - 1}"] + dbg_waits,
                inc="gd", k=DMA, mark=mk)

        s2 = ExitStack()
        S2 = s2.enter_context
        x2b = [S2(sb(f"x2b{i}", [128, D], F32)) for i in range(2)]
        q2 = [S2(sb(f"q2_{i}", [128, 1024], BF16)) for i in range(2)]
        dgq = [S2(sb(f"dgq{i}", [128, 16], F32)) for i in range(2)]
        U_sb = [S2(sb(f"U_sb{i}", [128, 256], BF16)) for i in range(2)]
        acol = [S2(sb(f"acol{i}", [128, 1], F32)) for i in range(2)]
        nst = [S2(sb(f"nst{i}", [128, 1], F32)) for i in range(2)]
        nd_sb = [S2(sb(f"nd_sb{i}", [128, NH * 65], F32)) for i in range(2)]
        dinv = [S2(sb(f"dinv{i}", [128, 16], F32)) for i in range(2)]
        att = [S2(sb(f"att{i}", [128, D], BF16)) for i in range(2)]
        attfm = [S2(sb(f"attfm{i}", [128, D], BF16)) for i in range(2)]
        attno = [S2(sb(f"attno{i}", [128, D], F32)) for i in range(2)]
        x1b = [S2(sb(f"x1b{i}", [128, D], F32)) for i in range(2)]
        ln2b = [S2(sb(f"ln2b{i}", [128, D], BF16)) for i in range(2)]
        mva = [S2(sb(f"mva{i}", [128, 2], F32)) for i in range(2)]
        mvb = [S2(sb(f"mvb{i}", [128, 2], F32)) for i in range(2)]
        sta = [S2(sb(f"sta{i}", [128, 12], F32)) for i in range(2)]
        rs1 = [S2(sb(f"rs1_{i}", [128, 1], F32)) for i in range(2)]
        nm1 = [S2(sb(f"nm1_{i}", [128, 1], F32)) for i in range(2)]
        rs2 = [S2(sb(f"rs2_{i}", [128, 1], F32)) for i in range(2)]
        nm2 = [S2(sb(f"nm2_{i}", [128, 1], F32)) for i in range(2)]

        psB = ExitStack()
        pA = [psB.enter_context(nc.psum_tensor(f"pA{i}", [128, 512], F32)) for i in range(2)]
        pB = [psB.enter_context(nc.psum_tensor(f"pB{i}", [128, 256], F32)) for i in range(2)]
        pT2 = psB.enter_context(nc.psum_tensor("pT2", [128, 1024], BF16))
        pWO = [psB.enter_context(nc.psum_tensor(f"pWO{i}", [128, 512], F32)) for i in range(2)]

        ps_prevA = [f"epscs_{NH - 1}", f"ctxs_{NH - 1}_1", "vsev"]

        for c in range(NCH):
            t = c // 4
            cc4 = c % 4
            pc = c % 2
            b.op("sync", lambda sy, c=c, pc=pc: sy.dma_start(
                x2b[pc].ap(), x_d.ap()[128 * c:128 * (c + 1), :]),
                waits=[b.ev(f"x1_{c - 2}")] + (ps_prevA if c == 0 else []),
                inc="sd", k=DMA, mark=f"x2_{c}")
            b.op("scalar", lambda s, pc=pc, t=t, cc4=cc4: s.activation(
                q2[pc].ap().rearrange("p (fo w) -> p fo w", fo=8),
                q_store.ap().rearrange("p (fo n) -> p fo n", fo=8)[:, :, t * 512 + cc4 * 128: t * 512 + (cc4 + 1) * 128],
                AF.Square, bias=0.0, scale=1.0),
                waits=["qkfree_7_7", b.ev(f"dgmmq_{c - 2}"), "asc"],
                inc="ss", mark=f"q2_{c}")
            for fc in range(8):
                b.op("tensor", lambda t_, pc=pc, fc=fc: t_.matmul(
                    pA[pc].ap()[:, 272 + 2 * fc: 274 + 2 * fc],
                    q2[pc].ap()[:, fc * 128:(fc + 1) * 128],
                    e2.ap(), start=True, stop=True),
                    waits=([f"q2_{c}", b.ev(f"ndev_{c - 1}_14"),
                            b.ev(f"ndev_{c - 1}_15")] + (ps_prevA if c == 0 else [])
                           if fc == 0 else []),
                    inc="st", mark=(f"dgmmq_{c}" if fc == 7 else None))
            b.op("vector", lambda v, pc=pc: v.tensor_scalar_mul(
                dgq[pc].ap(), pA[pc].ap()[:, 272:288], -1.0),
                waits=[f"dgmmq_{c}", b.ev(f"acq_{c - 2}_15")],
                inc="sv", mark=f"negdq_{c}")
            for h in range(NH):
                ph = h % 2
                hb = 64 * (h % 2)
                fo = h // 2
                qcol = fo * N_TOK + t * 512 + cc4 * 128
                b.op("tensor", lambda t_, ph=ph, hb=hb, qcol=qcol: t_.matmul(
                    pA[ph].ap()[:, 0:256],
                    q_store.ap()[hb:hb + 64, qcol:qcol + 128],
                    projcT2.ap()[hb:hb + 64, :], start=True, stop=True),
                    waits=[f"negdq_{c}", b.ev(f"ndev_{c}_{h - 2}")],
                    inc="st", mark=f"qdtm_{c}_{h}")
                b.op("vector", lambda v, ph=ph: v.reduce_max(
                    nst[ph].ap(), pA[ph].ap()[:, 0:256], AX.X, negate=True),
                    waits=[f"qdtm_{c}_{h}", b.ev(f"acq_{c}_{h - 2}")],
                    inc="sv", mark=f"stab_{c}_{h}")
                b.op("scalar", lambda s, ph=ph, pc=pc, h=h: s.activation(
                    acol[ph].ap(), nst[ph].ap(), AF.Exp,
                    bias=dgq[pc].ap()[:, h:h + 1], scale=1.0),
                    waits=[f"stab_{c}_{h}", b.ev(f"ndev_{c}_{h - 2}")],
                    inc="ss", mark=f"acq_{c}_{h}")
                for mc in range(2):
                    b.op("tensor", lambda t_, ph=ph, hb=hb, qcol=qcol, mc=mc: t_.matmul(
                        pB[ph].ap()[:, mc * 128:(mc + 1) * 128],
                        projcT2.ap()[hb:hb + 64, mc * 128:(mc + 1) * 128],
                        q_store.ap()[hb:hb + 64, qcol:qcol + 128],
                        start=True, stop=True),
                        waits=([b.ev(f"Uev_{c}_{h - 2}")] +
                               (ps_prevA if c == 0 and h < 2 else [])
                               if mc == 0 else []),
                        inc="st", mark=(f"qdm_{c}_{h}" if mc == 1 else None))
                b.op("scalar", lambda s, ph=ph: s.activation(
                    U_sb[ph].ap(), pB[ph].ap(), AF.Exp, bias=0.0, scale=1.0),
                    waits=[f"qdm_{c}_{h}", b.ev(f"ndmm_{c}_{h - 2}")],
                    inc="ss", mark=f"Uev_{c}_{h}")
                for mc in range(2):
                    b.op("tensor", lambda t_, ph=ph, h=h, mc=mc: t_.matmul(
                        pA[ph].ap()[:, 320:385],
                        U_sb[ph].ap()[:, mc * 128:(mc + 1) * 128],
                        CTXbf.ap()[:, h * 130 + mc * 65: h * 130 + (mc + 1) * 65],
                        start=(mc == 0), stop=(mc == 1)),
                        waits=([f"Uev_{c}_{h}", f"stab_{c}_{h}"] if mc == 0 else []),
                        inc="st", mark=(f"ndmm_{c}_{h}" if mc == 1 else None))
                b.op("vector", lambda v, ph=ph, pc=pc, h=h: v.scalar_tensor_tensor(
                    nd_sb[pc].ap()[:, h * 65:(h + 1) * 65],
                    pA[ph].ap()[:, 320:385], acol[ph].ap(),
                    epscs.ap()[:, h * 65:(h + 1) * 65], ALU.mult, ALU.add),
                    waits=[f"ndmm_{c}_{h}", f"acq_{c}_{h}",
                           b.ev(f"attx_{c - 2}")],
                    inc="sv", mark=f"ndev_{c}_{h}")
            b.op("vector", lambda v, pc=pc: v.reciprocal(
                dinv[pc].ap(),
                nd_sb[pc].ap().rearrange("p (h w) -> p h w", h=NH)[:, :, 64:65].squeeze(2)),
                waits=[f"ndev_{c}_{NH - 1}"], inc="svv", mark=f"dinv_{c}")
            for h in range(NH):
                b.op("scalar", lambda s, pc=pc, h=h: s.activation(
                    att[pc].ap()[:, h * 64:(h + 1) * 64],
                    nd_sb[pc].ap()[:, h * 65: h * 65 + 64],
                    AF.Copy, bias=0.0, scale=dinv[pc].ap()[:, h:h + 1]),
                    waits=([f"dinv_{c}", b.ev(f"attT_{c - 2}")] if h == 0 else []),
                    inc="ss", mark=(f"attx_{c}" if h == NH - 1 else None))
            for kc in range(8):
                b.op("tensor", lambda t_, pc=pc, kc=kc: t_.transpose(
                    pT2.ap()[:, kc * 128:(kc + 1) * 128],
                    att[pc].ap()[:, kc * 128:(kc + 1) * 128], idn_bf.ap()),
                    waits=([f"attx_{c}", b.ev(f"attTev_{c - 1}")] +
                           (ps_prevA if c == 0 else []) if kc == 0 else []),
                    inc="st", mark=(f"attT_{c}" if kc == 7 else None))
            if debug and c == 0:
                b.op("sync", lambda sy: sy.dma_start(dbg_att.ap(), att[0].ap()),
                     waits=["attx_0"], inc="od", k=DMA, mark="dbgatt")
                b.op("sync", lambda sy: sy.dma_start(dbg_nd.ap(), nd_sb[0].ap()),
                     waits=[f"ndev_0_{NH - 1}"], inc="od", k=DMA, mark="dbgnd")
            b.op("scalar", lambda s, pc=pc: s.activation(
                attfm[pc].ap(), pT2.ap(), AF.Copy, bias=0.0, scale=1.0),
                waits=[f"attT_{c}", b.ev(f"wops_{c - 2}_1")],
                inc="ss", mark=f"attTev_{c}")
            for half in range(2):
                for kc in range(8):
                    b.op("tensor", lambda t_, pc=pc, half=half, kc=kc: t_.matmul(
                        pWO[half].ap(),
                        attfm[pc].ap()[:, kc * 128:(kc + 1) * 128],
                        wo_sb.ap()[:, kc, half * 512:(half + 1) * 512],
                        start=(kc == 0), stop=(kc == 7)),
                        waits=([f"attTev_{c}", "w_wo",
                                b.ev(f"attno_{c - 1}_{half}")] +
                               (ps_prevA if c == 0 else []) if kc == 0 else []),
                        inc="st", mark=(f"wops_{c}_{half}" if kc == 7 else None))
                b.op("vector", lambda v, pc=pc, half=half: v.tensor_add(
                    attno[pc].ap()[:, half * 512:(half + 1) * 512],
                    pWO[half].ap(), bo_sb.ap()[:, half * 512:(half + 1) * 512]),
                    waits=[f"wops_{c}_{half}", "bo_bc",
                           b.ev(f"x1_{c - 2}")],
                    inc="sv", mark=f"attno_{c}_{half}")
            if debug and c == 0:
                b.op("sync", lambda sy: sy.dma_start(dbg_attno.ap(), attno[0].ap()),
                     waits=["attno_0_1"], inc="od", k=DMA, mark="dbgattno")
            b.op("vector", lambda v, pc=pc: v.bn_stats(
                sta[pc].ap()[:, 0:6], attno[pc].ap()[:, 0:512]),
                waits=[f"attno_{c}_1", b.ev(f"nm1_{c - 2}"), b.ev(f"mvb_{c - 2}")], inc="svv", mark=f"bsa_{c}")
            b.op("vector", lambda v, pc=pc: v.bn_stats(
                sta[pc].ap()[:, 6:12], attno[pc].ap()[:, 512:1024]),
                inc="svv", mark=f"bsa2_{c}")
            b.op("vector", lambda v, pc=pc: v.bn_aggr(mva[pc].ap(), sta[pc].ap()),
                 waits=[f"bsa2_{c}"], inc="sv", mark=f"mva_{c}")
            b.op("scalar", lambda s, pc=pc: s.activation(
                rs1[pc].ap(), mva[pc].ap()[:, 1:2], AF.Ln, bias=LN_EPS, scale=1.0),
                waits=[f"mva_{c}", b.ev(f"y1_{c - 2}")], inc="sss", mark=f"l1_{c}")
            b.op("scalar", lambda s, pc=pc: s.activation(
                rs1[pc].ap(), rs1[pc].ap(), AF.Exp, bias=0.0, scale=-0.5),
                waits=[f"l1_{c}"], inc="ss", mark=f"rs1_{c}")
            b.op("vector", lambda v, pc=pc: v.tensor_scalar(
                nm1[pc].ap(), mva[pc].ap()[:, 0:1], rs1[pc].ap(), -1.0,
                ALU.mult, ALU.mult),
                waits=[f"rs1_{c}", f"mva_{c}"], inc="sv", mark=f"nm1_{c}")
            b.op("scalar", lambda s, pc=pc: s.activation(
                attno[pc].ap(), attno[pc].ap(), AF.Identity,
                bias=nm1[pc].ap(), scale=rs1[pc].ap()),
                waits=[f"nm1_{c}"] + (["dbgattno"] if debug and c == 0 else []),
                inc="ss", mark=f"y1_{c}")
            b.op("vector", lambda v, pc=pc: v.tensor_add(
                x1b[pc].ap(), x2b[pc].ap(), attno[pc].ap()),
                waits=[f"y1_{c}", f"x2_{c}",
                       b.ev(f"x1w_{c - 2}"), b.ev(f"ln2s_{c - 2}")],
                inc="sv", mark=f"x1_{c}")
            b.op("vector", lambda v, pc=pc: v.bn_stats(
                sta[pc].ap()[:, 0:6], x1b[pc].ap()[:, 0:512]),
                waits=[f"x1_{c}"], inc="svv", mark=f"bsb_{c}")
            b.op("vector", lambda v, pc=pc: v.bn_stats(
                sta[pc].ap()[:, 6:12], x1b[pc].ap()[:, 512:1024]),
                inc="svv", mark=f"bsb2_{c}")
            b.op("vector", lambda v, pc=pc: v.bn_aggr(mvb[pc].ap(), sta[pc].ap()),
                 waits=[f"bsb2_{c}", b.ev(f"nm2_{c - 2}")], inc="sv", mark=f"mvb_{c}")
            b.op("scalar", lambda s, pc=pc: s.activation(
                rs2[pc].ap(), mvb[pc].ap()[:, 1:2], AF.Ln, bias=LN_EPS, scale=1.0),
                waits=[f"mvb_{c}"], inc="sss", mark=f"l2_{c}")
            b.op("scalar", lambda s, pc=pc: s.activation(
                rs2[pc].ap(), rs2[pc].ap(), AF.Exp, bias=0.0, scale=-0.5),
                waits=[f"l2_{c}"], inc="ss", mark=f"rs2_{c}")
            b.op("vector", lambda v, pc=pc: v.tensor_scalar(
                nm2[pc].ap(), mvb[pc].ap()[:, 0:1], rs2[pc].ap(), -1.0,
                ALU.mult, ALU.mult),
                waits=[f"rs2_{c}", f"mvb_{c}"], inc="sv", mark=f"nm2_{c}")
            b.op("scalar", lambda s, pc=pc: s.activation(
                ln2b[pc].ap(), x1b[pc].ap(), AF.Identity,
                bias=nm2[pc].ap(), scale=rs2[pc].ap()),
                waits=[f"nm2_{c}", b.ev(f"ln2w_{c - 2}")],
                inc="ss", mark=f"ln2s_{c}")
            b.op("scalar", lambda s, pc=pc, c=c: s.dma_start(
                x1_scr.ap()[128 * c:128 * (c + 1), :], x1b[pc].ap()),
                waits=[f"x1_{c}"], inc="od", k=DMA, mark=f"x1w_{c}")
            b.op("scalar", lambda s, pc=pc, c=c: s.dma_start(
                ln2_scr.ap()[128 * c:128 * (c + 1), :], ln2b[pc].ap()),
                waits=[f"ln2s_{c}"], inc="od", k=DMA, mark=f"ln2w_{c}")

        s2_done = [f"ln2w_{NCH - 1}", f"x1w_{NCH - 1}", f"ndev_{NCH - 1}_{NH - 1}",
                   f"attTev_{NCH - 1}", f"Uev_{NCH - 1}_{NH - 1}",
                   f"attno_{NCH - 1}_1"]

        if stop == "s2":
            psB.close()
            s2.close()
            attn_sc.close()
            f1wA.close()
            return _finalize()

        s2.close()
        attn_sc.close()
        psB.close()

        f1wB = ExitStack()
        w1_sb = f1wB.enter_context(sb("w1_sb", [128, 8, H2], BF16, side="right"))
        wg_sb = f1wB.enter_context(sb("wg_sb", [128, 8, E], BF16, side="right"))
        for (dst, src, mk) in ((w1_sb, w1_d, "w_w1"), (wg_sb, wg_d, "w_wg")):
            b.op("gpsimd", lambda g, d=dst, s=src: g.dma_start(
                d.ap(), s.ap().rearrange("(kc p) n -> p kc n", p=128)),
                waits=s2_done, inc="gd", k=DMA, mark=mk)

        f1 = ExitStack()
        F1 = f1.enter_context
        ln2in = [F1(sb(f"ln2in{i}", [128, 4, D], BF16)) for i in range(2)]
        ln2fm = [F1(sb(f"ln2fm{i}", [128, 8 * 512], BF16)) for i in range(2)]
        silu_t = [F1(sb(f"silu{i}", [128, 512], F32)) for i in range(2)]
        s1_sb = F1(sb("s1_sb", [128, 8 * 512], BF16))
        hid_sb = F1(sb("hid_sb", [128, 8 * 512], BF16))
        so1b = [F1(sb(f"so1b{i}", [128, D], F32)) for i in range(2)]

        psC = ExitStack()
        pG = [psC.enter_context(nc.psum_tensor(f"pG{i}", [128, 512], F32)) for i in range(4)]
        pS = [psC.enter_context(nc.psum_tensor(f"pS{i}", [128, 512], F32)) for i in range(2)]
        pT3 = psC.enter_context(nc.psum_tensor("pT3", [128, 1024], BF16))
        pL = psC.enter_context(nc.psum_tensor("pL", [128, 16], F32))

        ps_prevB = s2_done

        def pair_release_events(gidx):
            if gidx < 2:
                return []
            tt, rem = divmod(gidx - 2, 16)
            tag = "s" if rem < 8 else "h"
            i = rem % 8
            return [f"{tag}sl_{tt}_{i}", f"{tag}m_{tt}_{i}"]

        for t in range(NTL):
            p = t % 2
            b.op("sync", lambda sy, t=t, p=p: sy.dma_start(
                ln2in[p].ap(),
                ln2_scr.ap()[512 * t:512 * (t + 1), :].rearrange("(g q) d -> q g d", q=128)),
                waits=[f"ln2w_{4 * t + 3}", b.ev(f"l2T_{4 * (t - 2) + 3}")] +
                      (s2_done if t == 0 else []),
                inc="sd", k=DMA, mark=f"l2in_{t}")
            for cc4 in range(4):
                c = 4 * t + cc4
                for kc in range(8):
                    b.op("tensor", lambda t_, p=p, cc4=cc4, kc=kc: t_.transpose(
                        pT3.ap()[:, kc * 128:(kc + 1) * 128],
                        ln2in[p].ap()[:, cc4, kc * 128:(kc + 1) * 128], idn_bf.ap()),
                        waits=([f"l2in_{t}", b.ev(f"l2fm_{c - 1}")] +
                               (ps_prevB if c == 0 else []) if kc == 0 else []),
                        inc="st", mark=(f"l2T_{c}" if kc == 7 else None))
                b.op("scalar", lambda s, p=p, cc4=cc4: s.activation(
                    ln2fm[p].ap().rearrange("q (kc w) -> q kc w", kc=8)[:, :, cc4 * 128:(cc4 + 1) * 128],
                    pT3.ap().rearrange("q (kc w) -> q kc w", kc=8),
                    AF.Copy, bias=0.0, scale=1.0),
                    waits=[f"l2T_{c}", b.ev(f"lgps_{4 * (t - 2) + 3}")],
                    inc="ss", mark=f"l2fm_{c}")
            for (wsb, wev, dst, tag, goff) in ((wsi_sb, "w_wsi", s1_sb, "s", 0),
                                               (w1_sb, "w_w1", hid_sb, "h", 8)):
                for i in range(8):
                    gidx = 16 * t + goff + i
                    j = 2 * (gidx % 2)
                    rel = pair_release_events(gidx)
                    for (jj, fo) in ((j, i), (j + 1, i + 8)):
                        w_first = [f"l2fm_{4 * t + 3}", wev]
                        if rel:
                            w_first.append(rel[0] if fo == i else rel[1])
                        if gidx == 0:
                            w_first += ps_prevB
                        for kc in range(8):
                            b.op("tensor", lambda t_, p=p, jj=jj, fo=fo, kc=kc, wsb=wsb: t_.matmul(
                                pG[jj].ap(),
                                wsb.ap()[:, kc, fo * 128:(fo + 1) * 128],
                                ln2fm[p].ap()[:, kc * 512:(kc + 1) * 512],
                                start=(kc == 0), stop=(kc == 7)),
                                waits=(w_first if kc == 0 else []),
                                inc="st",
                                mark=((f"{tag}ga_{t}_{i}" if fo == i else f"{tag}gb_{t}_{i}")
                                      if kc == 7 else None))
                    b.op("scalar", lambda s, j=j, pp=i % 2: s.activation(
                        silu_t[pp].ap(), pG[j].ap(), AF.Silu, bias=0.0, scale=1.0),
                        waits=[f"{tag}ga_{t}_{i}"] + ([rel[1]] if rel else []),
                        inc="ss", mark=f"{tag}sl_{t}_{i}")
                    hid_extra = []
                    if tag == "s" and i == 0:
                        hid_extra = [b.ev(f"so1ps_{4 * (t - 1) + 3}_1")]
                    if tag == "h" and i == 0:
                        hid_extra = [b.ev(f"hidw_{t - 1}")]
                    b.op("vector", lambda v, j=j, pp=i % 2, i=i, dst=dst: v.tensor_mul(
                        dst.ap()[:, i * 512:(i + 1) * 512],
                        silu_t[pp].ap(), pG[j + 1].ap()),
                        waits=[f"{tag}sl_{t}_{i}", f"{tag}gb_{t}_{i}"] + hid_extra,
                        inc="sv", mark=f"{tag}m_{t}_{i}")
            for cc4 in range(4):
                c = 4 * t + cc4
                pc = c % 2
                for half in range(2):
                    for kc in range(8):
                        b.op("tensor", lambda t_, cc4=cc4, half=half, kc=kc: t_.matmul(
                            pS[half].ap(),
                            s1_sb.ap()[:, kc * 512 + cc4 * 128: kc * 512 + (cc4 + 1) * 128],
                            wso_sb.ap()[:, kc, half * 512:(half + 1) * 512],
                            start=(kc == 0), stop=(kc == 7)),
                            waits=([f"sm_{t}_7", "w_wso",
                                    b.ev(f"so1ev_{c - 1}_{half}")] if kc == 0 else []),
                            inc="st", mark=(f"so1ps_{c}_{half}" if kc == 7 else None))
                    b.op("scalar", lambda s, pc=pc, half=half: s.activation(
                        so1b[pc].ap()[:, half * 512:(half + 1) * 512], pS[half].ap(),
                        AF.Copy, bias=0.0, scale=1.0),
                        waits=[f"so1ps_{c}_{half}", b.ev(f"so1w_{c - 2}")],
                        inc="ss", mark=f"so1ev_{c}_{half}")
                b.op("scalar", lambda s, pc=pc, c=c: s.dma_start(
                    so1_scr.ap()[128 * c:128 * (c + 1), :], so1b[pc].ap()),
                    waits=[f"so1ev_{c}_1"], inc="od", k=DMA, mark=f"so1w_{c}")
                for kc in range(8):
                    b.op("tensor", lambda t_, p=p, cc4=cc4, pc=pc, kc=kc: t_.matmul(
                        pL.ap()[:, pc * 8:(pc + 1) * 8],
                        ln2fm[p].ap()[:, kc * 512 + cc4 * 128: kc * 512 + (cc4 + 1) * 128],
                        wg_sb.ap()[:, kc, :],
                        start=(kc == 0), stop=(kc == 7)),
                        waits=([f"l2fm_{c}", "w_wg",
                                b.ev(f"lgev_{c - 1}")] if kc == 0 else []),
                        inc="st", mark=(f"lgps_{c}" if kc == 7 else None))
                b.op("vector", lambda v, pc=pc, c=c: v.tensor_add(
                    logits_sb.ap()[:, c * 8:(c + 1) * 8],
                    pL.ap()[:, pc * 8:(pc + 1) * 8], gb_sb.ap()),
                    waits=[f"lgps_{c}", "gb_bc"], inc="sv", mark=f"lgev_{c}")
            b.op("scalar", lambda s, t=t: s.dma_start(
                hid_scr.ap()[:, :, 512 * t:512 * (t + 1)],
                hid_sb.ap().rearrange("p (kc w) -> p kc w", kc=8)),
                waits=[f"hm_{t}_7"], inc="od", k=DMA, mark=f"hidw_{t}")

        ga_sc = ExitStack()
        GA = ga_sc.enter_context
        E_sb = GA(sb("E_sb", [128, NCH * E], F32))
        gS = GA(sb("gS", [128, NCH], F32))
        gInv = GA(sb("gInv", [128, NCH], F32))
        gM1 = GA(sb("gM1", [128, NCH], F32))
        gM2 = GA(sb("gM2", [128, NCH], F32))
        gEq = GA(sb("gEq", [128, NCH * E], F32))
        gGe = GA(sb("gGe", [128, NCH * E], F32))

        b.op("scalar", lambda s: s.activation(
            E_sb.ap(), logits_sb.ap(), AF.Exp, bias=0.0, scale=1.0),
            waits=[f"lgev_{NCH - 1}"], inc="ss", mark="gE")
        for c in range(NCH):
            b.op("vector", lambda v, c=c: v.reduce_sum(
                gS.ap()[:, c:c + 1], E_sb.ap()[:, c * 8:(c + 1) * 8], AX.X),
                waits=(["gE"] if c == 0 else []), inc="svv",
                mark=("gSum" if c == NCH - 1 else None))
        b.op("vector", lambda v: v.reciprocal(gInv.ap(), gS.ap()),
             waits=["gSum"], inc="svv", mark="gInv")
        for c in range(NCH):
            b.op("vector", lambda v, c=c: v.reduce_max(
                gM1.ap()[:, c:c + 1], E_sb.ap()[:, c * 8:(c + 1) * 8], AX.X),
                inc="svv", mark=("gM1" if c == NCH - 1 else None))
        for c in range(NCH):
            b.op("vector", lambda v, c=c: v.tensor_scalar(
                gEq.ap()[:, c * 8:(c + 1) * 8], E_sb.ap()[:, c * 8:(c + 1) * 8],
                gM1.ap()[:, c:c + 1], None, ALU.is_equal),
                waits=(["gM1"] if c == 0 else []), inc="sv",
                mark=("geq" if c == NCH - 1 else None))
        for c in range(NCH):
            b.op("vector", lambda v, c=c: v.scalar_tensor_tensor(
                gEq.ap()[:, c * 8:(c + 1) * 8], gEq.ap()[:, c * 8:(c + 1) * 8],
                -1e30, E_sb.ap()[:, c * 8:(c + 1) * 8], ALU.mult, ALU.add),
                waits=(["geq"] if c == 0 else []), inc="sv",
                mark=(f"gmsk" if c == NCH - 1 else None))
        for c in range(NCH):
            b.op("vector", lambda v, c=c: v.reduce_max(
                gM2.ap()[:, c:c + 1], gEq.ap()[:, c * 8:(c + 1) * 8], AX.X),
                waits=(["gmsk"] if c == 0 else []), inc="svv",
                mark=("gM2" if c == NCH - 1 else None))
        for c in range(NCH):
            b.op("vector", lambda v, c=c: v.tensor_scalar(
                gGe.ap()[:, c * 8:(c + 1) * 8], E_sb.ap()[:, c * 8:(c + 1) * 8],
                gM2.ap()[:, c:c + 1], None, ALU.is_ge),
                waits=(["gM2"] if c == 0 else []), inc="sv",
                mark=("gge" if c == NCH - 1 else None))
        b.op("vector", lambda v: v.tensor_mul(gGe.ap(), gGe.ap(), E_sb.ap()),
             waits=["gge"], inc="sv", mark="gmul")
        for c in range(NCH):
            b.op("vector", lambda v, c=c: v.tensor_scalar_mul(
                wdense_sb.ap()[:, c * 8:(c + 1) * 8], gGe.ap()[:, c * 8:(c + 1) * 8],
                gInv.ap()[:, c:c + 1]),
                waits=(["gInv", "gmul"] if c == 0 else []), inc="sv",
                mark=("wdense" if c == NCH - 1 else None))

        if stop == "f1":
            ga_sc.close()
            psC.close()
            f1.close()
            f1wB.close()
            f1wA.close()
            return _finalize()

        f1_done = [f"l2fm_{NCH - 1}", f"hm_{NTL - 1}_7", f"hsl_{NTL - 1}_7",
                   f"so1ev_{NCH - 1}_1", f"lgev_{NCH - 1}", "wdense",
                   f"hidw_{NTL - 1}", f"so1w_{NCH - 1}"]

        ga_sc.close()
        f1.close()
        psC.close()
        f1wB.close()
        f1wA.close()

        f2w = ExitStack()
        w2_sb = f2w.enter_context(sb("w2_sb", [128, 64, D], BF16))
        for e in range(E):
            b.op("gpsimd", lambda g, e=e: g.dma_start(
                w2_sb.ap()[:, 8 * e:8 * (e + 1), :],
                w2_d.ap().rearrange("e (kc p) n -> p (e kc) n", p=128)[:, 8 * e:8 * (e + 1), :]),
                waits=s2_done + f1_done, inc="gd", k=DMA, mark=f"w2_{e}")

        if stop == "w2":
            f2w.close()
            return _finalize()


        f2 = ExitStack()
        F2 = f2.enter_context
        hid_in = [F2(sb(f"hidin{i}", [128, 8, 512], BF16)) for i in range(2)]
        x1in = [F2(sb(f"x1in{i}", [128, D], F32)) for i in range(2)]
        so1in = [F2(sb(f"so1in{i}", [128, D], F32)) for i in range(2)]
        facc = [F2(sb(f"facc{i}", [128, D], F32)) for i in range(2)]
        y2b = [F2(sb(f"y2b{i}", [128, D], F32)) for i in range(2)]
        outb = [F2(sb(f"outb{i}", [128, D], F32)) for i in range(2)]
        mvf = [F2(sb(f"mvf{i}", [128, 2], F32)) for i in range(2)]
        mvf2 = [F2(sb(f"mvf2_{i}", [128, 2], F32)) for i in range(2)]
        fin = [F2(sb(f"fin{i}", [128, D], F32)) for i in range(2)]
        stf = [F2(sb(f"stf{i}", [128, 12], F32)) for i in range(2)]
        rsf = [F2(sb(f"rsf{i}", [128, 1], F32)) for i in range(2)]
        nmf = [F2(sb(f"nmf{i}", [128, 1], F32)) for i in range(2)]

        psD = ExitStack()
        pE = [psD.enter_context(nc.psum_tensor(f"pE{i}", [128, 512], F32)) for i in range(8)]

        for t in range(NTL):
            p = t % 2
            b.op("sync", lambda sy, t=t, p=p: sy.dma_start(
                hid_in[p].ap(), hid_scr.ap()[:, :, 512 * t:512 * (t + 1)]),
                waits=[f"hidw_{t}",
                       b.ev(f"emm_{4 * (t - 2) + 3}_7_1")] +
                      (f1_done if t == 0 else []),
                inc="sd", k=DMA, mark=f"hin_{t}")
            for cc4 in range(4):
                c = 4 * t + cc4
                pc = c % 2
                b.op("sync", lambda sy, c=c, pc=pc: sy.dma_start(
                    so1in[pc].ap(), so1_scr.ap()[128 * c:128 * (c + 1), :]),
                    waits=[f"so1w_{c}", b.ev(f"fch_{c - 2}_0_1")] +
                          (f1_done if c == 0 else []),
                    inc="sd", k=DMA, mark=f"so1in_{c}")
                for grp in range(2):
                    if F2_MODE == "nomm":
                        continue
                    for e in range(4 * grp, 4 * grp + 4):
                        for half in range(2):
                            bank = (e % 4) * 2 + half
                            for kc in range(8):
                                b.op("tensor", lambda t_, p=p, cc4=cc4, half=half, e=e, kc=kc, bank=bank: t_.matmul(
                                    pE[bank].ap(),
                                    hid_in[p].ap()[:, kc, cc4 * 128:(cc4 + 1) * 128],
                                    w2_sb.ap()[:, e * 8 + kc, half * 512:(half + 1) * 512],
                                    start=(kc == 0), stop=(kc == 7)),
                                    waits=([f"hin_{t}", f"w2_{e}",
                                            b.ev(f"fch2_{c - 1}_{e % 4}_{half}" if grp == 0
                                                 else f"fch_{c}_{e % 4}_{half}")] +
                                           (f1_done if c == 0 and grp == 0 else [])
                                           if kc == 0 else []),
                                    inc="st", mark=(f"emm_{c}_{e}_{half}" if kc == 7 else None))
                    for e in range(4 * grp, 4 * grp + 4):
                        for half in range(2):
                            bank = (e % 4) * 2 + half
                            first = (e % 4 == 0 and grp == 0)
                            if first:
                                prevm = None
                            elif e % 4 == 0:
                                prevm = f"fch_{c}_3_{half}"
                            elif e < 4:
                                prevm = f"fch_{c}_{e % 4 - 1}_{half}"
                            else:
                                prevm = f"fch2_{c}_{e % 4 - 1}_{half}"
                            if F2_MODE == "nochain":
                                fchfn = lambda v, pc=pc, half=half, bank=bank: v.tensor_copy(
                                    facc[pc].ap()[:, half * 512:(half + 1) * 512], pE[bank].ap())
                            else:
                                fchfn = lambda v, pc=pc, half=half, e=e, c=c, bank=bank, first=first: v.scalar_tensor_tensor(
                                    facc[pc].ap()[:, half * 512:(half + 1) * 512],
                                    pE[bank].ap(), wdense_sb.ap()[:, c * 8 + e: c * 8 + e + 1],
                                    (so1in[pc].ap()[:, half * 512:(half + 1) * 512] if first
                                     else facc[pc].ap()[:, half * 512:(half + 1) * 512]),
                                    ALU.mult, ALU.add)
                            b.op("vector", fchfn,
                                waits=[f"emm_{c}_{e}_{half}", "wdense", prevm] +
                                      ([f"so1in_{c}", b.ev(f"fow_{c - 2}")] if first else []),
                                inc="sv", mark=f"fch_{c}_{e % 4}_{half}" if e < 4 else f"fch2_{c}_{e % 4}_{half}")
                if F2_MODE == "mmonly":
                    continue
                if F2_MODE == "writemvf":
                    b.op("vector", lambda v, pc=pc: v.tensor_scalar_mul(
                        mvf[pc].ap(), x1in[pc].ap()[:, 0:2], 2.0),
                        waits=[f"x1in_{c}"], inc="sv", mark=f"wm_{c}")
                    b.op("vector", lambda v, pc=pc: v.tensor_scalar_mul(
                        stf[pc].ap(), x1in[pc].ap()[:, 0:12], 2.0),
                        inc="sv", mark=f"ws_{c}")
                    b.op("vector", lambda v, pc=pc: v.tensor_scalar_mul(
                        stf2[pc].ap(), x1in[pc].ap(), 2.0),
                        inc="sv", mark=f"ws2_{c}")
                if F2_MODE == "redx1":
                    b.op("vector", lambda v, pc=pc: v.reduce_sum(
                        mvf[pc].ap()[:, 0:1], x1in[pc].ap(), AX.X),
                        waits=[f"fch2_{c}_3_1", f"x1in_{c}"], inc="svv", mark=f"rx_{c}")
                    b.op("vector", lambda v, pc=pc: v.tensor_add(
                        outb[pc].ap(), x1in[pc].ap(), x1in[pc].ap()),
                        waits=[f"rx_{c}", b.ev(f"outw_{c - 2}")],
                        inc="sv", mark=f"outv_{c}")
                    b.op("sync", lambda sy, pc=pc, c=c: sy.dma_start(
                        y_d.ap()[128 * c:128 * (c + 1), :], outb[pc].ap()),
                        waits=[f"outv_{c}"], inc="od", k=DMA, mark=f"outw_{c}")
                    continue
                if F2_MODE == "readfacc2":
                    b.op("vector", lambda v, pc=pc: v.tensor_mul(
                        outb[pc].ap(), facc[pc].ap(), facc[pc].ap()),
                        waits=[f"fch2_{c}_3_1", f"x1in_{c}", b.ev(f"outw_{c - 2}")],
                        inc="sv", mark=f"outv_{c}")
                    b.op("sync", lambda sy, pc=pc, c=c: sy.dma_start(
                        y_d.ap()[128 * c:128 * (c + 1), :], outb[pc].ap()),
                        waits=[f"outv_{c}"], inc="od", k=DMA, mark=f"outw_{c}")
                    continue
                if F2_MODE in ("noln", "readfacc", "writemvf"):
                    b.op("vector", lambda v, pc=pc, m=F2_MODE: v.tensor_add(
                        outb[pc].ap(), x1in[pc].ap(),
                        facc[pc].ap() if m == "readfacc" else x1in[pc].ap()),
                        waits=[f"x1in_{c}", f"fch2_{c}_3_1", b.ev(f"outw_{c - 2}")],
                        inc="sv", mark=f"outv_{c}")
                    b.op("sync", lambda sy, pc=pc, c=c: sy.dma_start(
                        y_d.ap()[128 * c:128 * (c + 1), :], outb[pc].ap()),
                        waits=[f"outv_{c}"], inc="od", k=DMA, mark=f"outw_{c}")
                    continue
                b.op("scalar", lambda s, pc=pc, c=c: s.dma_start(
                    fstat_scr.ap()[128 * c:128 * (c + 1), :], facc[pc].ap()),
                    waits=[f"fch2_{c}_3_1", b.ev(f"fow_{c - 2}")],
                    inc="od", k=DMA, mark=f"fow_{c}")
            if stop == f"f2t{t}":
                f2.close()
                psD.close()
                f2w.close()
                return _finalize()

        b.op("tensor", lambda t_: t_.matmul(
            pE[0].ap()[0:1, 0:1], ones_mat.ap()[:, 0:1], ones_tok.ap(),
            start=True, stop=True),
            waits=[f"fow_{NCH - 1}", f"fch2_{NCH - 1}_3_1"],
            inc="st", mark="q_t")
        barrier = ["q_t", f"fow_{NCH - 1}", f"fch2_{NCH - 1}_3_1",
                   f"emm_{NCH - 1}_7_1", "wdense"]
        for c in range(NCH):
            pc = c % 2
            b.op("sync", lambda sy, c=c, pc=pc: sy.dma_start(
                fin[pc].ap(), fstat_scr.ap()[128 * c:128 * (c + 1), :]),
                waits=barrier + [f"fow_{c}", b.ev(f"y2_{c - 2}")],
                inc="sd", k=DMA, mark=f"fir_{c}")
            b.op("sync", lambda sy, c=c, pc=pc: sy.dma_start(
                x1in[pc].ap(), x1_scr.ap()[128 * c:128 * (c + 1), :]),
                waits=[f"x1w_{c}", b.ev(f"outv_{c - 2}")],
                inc="sd", k=DMA, mark=f"x1in_{c}")
            b.op("vector", lambda v, pc=pc: v.bn_stats(
                stf[pc].ap()[:, 0:6], fin[pc].ap()[:, 0:512]),
                waits=barrier + [f"fir_{c}", b.ev(f"nmf_{c - 2}")],
                inc="svv", mark=f"bsf_{c}")
            b.op("vector", lambda v, pc=pc: v.bn_stats(
                stf[pc].ap()[:, 6:12], fin[pc].ap()[:, 512:1024]),
                inc="svv", mark=f"bsf2_{c}")
            b.op("vector", lambda v, pc=pc: v.bn_aggr(mvf2[pc].ap(), stf[pc].ap()),
                 waits=[f"bsf2_{c}"], inc="sv", mark=f"mvf_{c}")
            if F2_MODE == "f3noln":
                b.op("vector", lambda v, pc=pc: v.tensor_add(
                    outb[pc].ap(), x1in[pc].ap(), x1in[pc].ap()),
                    waits=[f"mvf_{c}", f"x1in_{c}", b.ev(f"outw_{c - 2}")],
                    inc="sv", mark=f"outv_{c}")
                b.op("sync", lambda sy, pc=pc, c=c: sy.dma_start(
                    y_d.ap()[128 * c:128 * (c + 1), :], outb[pc].ap()),
                    waits=[f"outv_{c}"], inc="od", k=DMA, mark=f"outw_{c}")
                b.events[f"y2_{c}"] = b.events[f"outv_{c}"]
                b.events[f"nmf_{c}"] = b.events[f"outv_{c}"]
                continue
            b.op("scalar", lambda s, pc=pc: s.activation(
                rsf[pc].ap(), mvf2[pc].ap()[:, 1:2], AF.Ln, bias=LN_EPS, scale=1.0),
                waits=barrier + [f"mvf_{c}", b.ev(f"y2_{c - 2}")],
                inc="sss", mark=f"lf_{c}")
            b.op("scalar", lambda s, pc=pc: s.activation(
                rsf[pc].ap(), rsf[pc].ap(), AF.Exp, bias=0.0, scale=-0.5),
                waits=[f"lf_{c}"], inc="ss", mark=f"rsf_{c}")
            b.op("vector", lambda v, pc=pc: v.tensor_scalar(
                nmf[pc].ap(), mvf2[pc].ap()[:, 0:1], rsf[pc].ap(), -1.0,
                ALU.mult, ALU.mult),
                waits=[f"rsf_{c}", f"mvf_{c}"], inc="sv", mark=f"nmf_{c}")
            b.op("scalar", lambda s, pc=pc: s.activation(
                y2b[pc].ap(), fin[pc].ap(), AF.Identity,
                bias=nmf[pc].ap(), scale=rsf[pc].ap()),
                waits=[f"nmf_{c}"], inc="ss", mark=f"y2_{c}")
            b.op("vector", lambda v, pc=pc: v.tensor_add(
                outb[pc].ap(), x1in[pc].ap(), y2b[pc].ap()),
                waits=[f"y2_{c}", f"x1in_{c}", b.ev(f"outw_{c - 2}")],
                inc="sv", mark=f"outv_{c}")
            b.op("sync", lambda sy, pc=pc, c=c: sy.dma_start(
                y_d.ap()[128 * c:128 * (c + 1), :], outb[pc].ap()),
                waits=[f"outv_{c}"], inc="od", k=DMA, mark=f"outw_{c}")

        f2.close()
        psD.close()
        f2w.close()

        return _finalize()

    return nc


_NC_CACHE = None


def kernel(**inputs):
    global _NC_CACHE
    if _NC_CACHE is None:
        _NC_CACHE = build_nc()
    nc = _NC_CACHE
    x = np.ascontiguousarray(inputs["L0_attn_x"])
    weights = {k: np.ascontiguousarray(v) for k, v in inputs.items()
               if k != "L0_attn_x"}
    in_maps = []
    for i in range(N_CORES):
        m = {"L0_attn_x": np.ascontiguousarray(x[i])}
        m.update(weights)
        in_maps.append(m)
    res = run_bass_kernel_spmd(nc, in_maps, core_ids=list(range(N_CORES)))
    return np.stack([res.results[i]["y"] for i in range(N_CORES)], axis=0)
